# version 04
# baseline (speedup 1.0000x reference)
import numpy as np
from contextlib import ExitStack

import concourse.bass as bass
import concourse.mybir as mybir
from concourse.bass_utils import run_bass_kernel_spmd

F32 = mybir.dt.float32
BF16 = mybir.dt.bfloat16
I32 = mybir.dt.int32
AF = mybir.ActivationFunctionType
ALU = mybir.AluOpType
AX = mybir.AxisListType

T = 2048
D = 4096
NT = T // 128
FC = D // 128
G = 8
H = 32
DKV = 1024
EPS = 1e-5

COMPUTE = ("tensor", "vector", "scalar")
ENGINES = ("tensor", "vector", "scalar", "gpsimd", "sync")


class Sched:

    def __init__(self):
        self.tasks = []
        self.last_writer = {}
        self.readers = {}

    def add(self, engine, emit, reads=(), writes=(), inc=None):
        tid = len(self.tasks)
        if inc is None:
            inc = 16 if engine in ("sync", "gpsimd") else 1
        deps = set()
        for k in reads:
            w = self.last_writer.get(k)
            if w is not None:
                deps.add(w)
        for k in writes:
            w = self.last_writer.get(k)
            if w is not None:
                deps.add(w)
            for rr in self.readers.get(k, ()):
                deps.add(rr)
        deps.discard(tid)
        for k in reads:
            self.readers.setdefault(k, set()).add(tid)
        for k in writes:
            self.last_writer[k] = tid
            self.readers[k] = set()
        self.tasks.append((engine, inc, emit, deps))
        return tid

    def emit(self, nc, block, sems):
        cum = {e: 0 for e in ENGINES}
        info = []
        for (engine, inc, emit, deps) in self.tasks:
            cum[engine] += inc
            info.append((engine, cum[engine]))

        totals = dict(cum)

        def run_engine(name):
            def f(e):
                waited = {e2: 0 for e2 in ENGINES}
                for tid, (engine, inc, emit, deps) in enumerate(self.tasks):
                    if engine != name:
                        continue
                    need_drain = False
                    for d in sorted(deps):
                        deng, dcum = info[d]
                        if deng == name:
                            if name in COMPUTE:
                                need_drain = True
                            else:
                                if dcum > waited[deng]:
                                    waited[deng] = dcum
                                    e.wait_ge(sems[deng], dcum)
                        else:
                            if dcum > waited[deng]:
                                waited[deng] = dcum
                                e.wait_ge(sems[deng], dcum)
                    if need_drain:
                        e.drain()
                    last = emit(e)
                    last.then_inc(sems[name], inc)
                if name == "sync":
                    for e2 in ENGINES:
                        e.wait_ge(sems[e2], totals[e2])
            return f

        block.sync(run_engine("sync"))
        block.vector(run_engine("vector"))
        block.scalar(run_engine("scalar"))
        block.tensor(run_engine("tensor"))
        block.gpsimd(run_engine("gpsimd"))


def build_nc():
    nc = bass.Bass("TRN2", target_bir_lowering=False)

    x_h = nc.dram_tensor("x", [T, D], F32, kind="ExternalInput").ap()
    wqt_h = nc.dram_tensor("wqt", [32, 128, 4096], F32, kind="ExternalInput").ap()
    wkt_h = nc.dram_tensor("wkt", [8, 128, 4096], F32, kind="ExternalInput").ap()
    wvt_h = nc.dram_tensor("wvt", [8, 128, 4096], F32, kind="ExternalInput").ap()
    wot_h = nc.dram_tensor("wot", [32, 128, 4096], F32, kind="ExternalInput").ap()
    pt_h = nc.dram_tensor("pt", [32, 2, 128, 1536], I32, kind="ExternalInput").ap()
    bqv_h = nc.dram_tensor("bqv", [128, 32], F32, kind="ExternalInput").ap()
    bkv_h = nc.dram_tensor("bkv", [128, 8], F32, kind="ExternalInput").ap()
    bvv_h = nc.dram_tensor("bvv", [128, 8], F32, kind="ExternalInput").ap()
    bov_h = nc.dram_tensor("bov", [128, 32], F32, kind="ExternalInput").ap()
    scv_h = nc.dram_tensor("scv", [128, 32], F32, kind="ExternalInput").ap()
    bfv_h = nc.dram_tensor("bfv", [128, 32], F32, kind="ExternalInput").ap()
    zpr_h = nc.dram_tensor("zpr", [1, 4096], F32, kind="ExternalInput").ap()

    out_h = nc.dram_tensor("out", [T, D], F32, kind="ExternalOutput").ap()
    xn1_h = nc.dram_tensor("xn1_h", [T, D], BF16, kind="Internal").ap()
    qpt_h = nc.dram_tensor("qpt_h", [D, T], BF16, kind="Internal").ap()
    kpt_h = nc.dram_tensor("kpt_h", [DKV, T], BF16, kind="Internal").ap()
    vt_h = nc.dram_tensor("vt_h", [DKV, T], BF16, kind="Internal").ap()
    o_h = nc.dram_tensor("o_h", [D, T], BF16, kind="Internal").ap()
    x2_h = nc.dram_tensor("x2_h", [T, D], F32, kind="Internal").ap()
    xn2_h = nc.dram_tensor("xn2_h", [T, D], BF16, kind="Internal").ap()
    ffn_h = nc.dram_tensor("ffn_h", [D, T], BF16, kind="Internal").ap()

    with ExitStack() as stack:
        sb = lambda *a: stack.enter_context(nc.sbuf_tensor(*a))
        ps = lambda *a: stack.enter_context(nc.psum_tensor(*a))

        big = sb("big", [128, FC, T], BF16)
        b4 = [sb(f"b4_{u}", [128, 4096], BF16) for u in range(4)]
        f32p = [sb(f"f32p_{u}", [128, 4096], F32) for u in range(2)]
        pb_s = sb("pb", [128, 1536], I32)
        ub32 = sb("ub32", [128, 512], I32)
        nrs_s = None

        bn6 = [sb(f"bn6_{u}", [128, 48], F32) for u in range(2)]
        mvb = sb("mvb", [128, 64], F32)
        vpe = sb("vpe", [128, 32], F32)
        sdv = sb("sdv", [128, 32], F32)
        srb = sb("srb", [128, 32], F32)
        snm = sb("snm", [128, 32], F32)
        sones = sb("sones", [128, 1], BF16)
        bql = sb("bql", [128, 32], F32)
        bq1l = sb("bq1l", [128, 32], F32)
        bkl = sb("bkl", [128, 8], F32)
        bk1l = sb("bk1l", [128, 8], F32)
        bvl = sb("bvl", [128, 8], F32)
        bol = sb("bol", [128, 32], F32)
        scl = sb("scl", [128, 32], F32)
        bfl = sb("bfl", [128, 32], F32)
        nrm = sb("nrm", [128, 8], F32)
        rna = sb("rna", [128, 8], F32)
        rnb = sb("rnb", [128, 8], F32)
        kvt = sb("kvt", [128, 1024], BF16)

        nrs_s = b4[2][0:1, 2048:4096]

        pA = ps("pA", [128, 2048], F32)
        pB = ps("pB", [128, 2048], F32)
        psum = [pA, pB]

        sems = {e: stack.enter_context(nc.semaphore(f"sem_{e}")) for e in ENGINES}
        block = stack.enter_context(nc.Block())

        S = Sched()

        S.add("sync", lambda e: e.dma_start(out=bql[:], in_=bqv_h[:, :]), writes=[("bql",)])
        S.add("sync", lambda e: e.dma_start(out=bkl[:], in_=bkv_h[:, :]), writes=[("bkl",)])
        S.add("sync", lambda e: e.dma_start(out=bvl[:], in_=bvv_h[:, :]), writes=[("bvl",)])
        S.add("sync", lambda e: e.dma_start(out=bol[:], in_=bov_h[:, :]), writes=[("bol",)])
        S.add("sync", lambda e: e.dma_start(out=scl[:], in_=scv_h[:, :]), writes=[("scl",)])
        S.add("sync", lambda e: e.dma_start(out=bfl[:], in_=bfv_h[:, :]), writes=[("bfl",)])
        S.add("vector", lambda e: e.tensor_scalar(bq1l[:], bql[:], 1.0, None, ALU.add),
              reads=[("bql",)], writes=[("bq1l",)])
        S.add("vector", lambda e: e.tensor_scalar(bk1l[:], bkl[:], 1.0, None, ALU.add),
              reads=[("bkl",)], writes=[("bk1l",)])
        S.add("vector", lambda e: e.memset(sones[:], 1.0), writes=[("sones",)])

        def ln_stats(i, src_ap, src_keys, col, bnbuf):
            c = col
            def bn_fn(e):
                for g in range(8):
                    e.bn_stats(bn6[bnbuf][:, g*6:(g+1)*6], src_ap[:, g*512:(g+1)*512])
                e.drain()
                e.bn_aggr(mvb[:, 2*c:2*c+2], bn6[bnbuf][:])
                e.drain()
                return e.tensor_scalar(vpe[:, c:c+1], mvb[:, 2*c+1:2*c+2], EPS, None, ALU.add)
            S.add("vector", bn_fn, reads=src_keys,
                  writes=[("bn6", bnbuf), ("mvb", c), ("vpe", c)])
            S.add("scalar", lambda e: e.activation(sdv[:, c:c+1], vpe[:, c:c+1], AF.Sqrt),
                  reads=[("vpe", c)], writes=[("sdv", c)])
            def fin(e):
                e.reciprocal(srb[:, c:c+1], sdv[:, c:c+1])
                e.drain()
                return e.tensor_scalar(snm[:, c:c+1], mvb[:, 2*c:2*c+1],
                                       srb[:, c:c+1], -1.0, ALU.mult, ALU.mult)
            S.add("vector", fin, reads=[("sdv", c), ("mvb", c)],
                  writes=[("srb", c), ("snm", c)])

        for i in range(NT):
            xi = i % 2
            ui = 2 + (i % 2)
            S.add("sync", lambda e, i=i, xi=xi: e.dma_start(
                out=f32p[xi][:], in_=x_h[i*128:(i+1)*128, :]),
                reads=[], writes=[("f32p", xi)])
            ln_stats(i, f32p[xi][:], [("f32p", xi)], i, i % 2)
            S.add("scalar", lambda e, i=i, xi=xi, ui=ui: e.activation(
                b4[ui][:], f32p[xi][:], AF.Identity,
                bias=snm[:, i:i+1], scale=srb[:, i:i+1]),
                reads=[("f32p", xi), ("srb", i), ("snm", i)],
                writes=[("b4", ui, 0), ("b4", ui, 1)])
            S.add("sync", lambda e, i=i, ui=ui: e.dma_start(
                out=xn1_h[i*128:(i+1)*128, :], in_=b4[ui][:]),
                reads=[("b4", ui, 0), ("b4", ui, 1)], writes=[("xn1_h", i)])

        for hh in range(2):
            for f in range(FC):
                S.add("sync", lambda e, f=f, hh=hh: e.dma_start_transpose(
                    big[:, f, hh*1024:(hh+1)*1024],
                    xn1_h[hh*1024:(hh+1)*1024, f*128:(f+1)*128]),
                    reads=[("xn1_h", i) for i in range(hh*8, hh*8+8)],
                    writes=[("big", f)])

        def proj_tile(j, kind, nt):
            wsrc = {"q": wqt_h, "k": wkt_h, "v": wvt_h}[kind]
            wb = j % 2
            pq = psum[j % 2]
            pkey = ("psum", j % 2)
            S.add("gpsimd", lambda e: e.dma_start(out=b4[wb][:], in_=wsrc[nt, :, :]),
                  writes=[("b4", wb, 0), ("b4", wb, 1)])
            def mm(e):
                for k in range(FC):
                    for tb in range(4):
                        ii = e.matmul(pq[:, tb*512:(tb+1)*512],
                                      b4[wb][:, k*128:(k+1)*128],
                                      big[:, k, tb*512:(tb+1)*512],
                                      start=(k == 0), stop=(k == FC - 1))
                return ii
            S.add("tensor", mm,
                  reads=[("b4", wb, 0), ("b4", wb, 1)] + [("big", k) for k in range(FC)],
                  writes=[pkey])
            eqh = j % 2
            eq = b4[2][:, eqh*2048:(eqh+1)*2048]
            ekey = ("b4", 2, eqh)
            if kind == "v":
                S.add("scalar", lambda e: e.activation(
                    eq, pq[:], AF.Identity, bias=bvl[:, nt:nt+1], scale=1.0),
                    reads=[pkey, ("bvl",)], writes=[ekey])
            else:
                bl = bql if kind == "q" else bkl
                b1 = bq1l if kind == "q" else bk1l
                bcol = nt
                for hh in range(2):
                    tmin = f32p[0][:, hh*2048:hh*2048+1024]
                    texp = f32p[0][:, hh*2048+1024:hh*2048+2048]
                    pqs = pq[:, hh*1024:(hh+1)*1024]
                    S.add("vector", lambda e, tmin=tmin, pqs=pqs, bl=bl, bcol=bcol:
                          e.tensor_scalar(tmin, pqs, bl[:, bcol:bcol+1], 0.0, ALU.add, ALU.min),
                          reads=[pkey], writes=[("f32p", 0)])
                    S.add("scalar", lambda e, texp=texp, tmin=tmin:
                          e.activation(texp, tmin, AF.Exp),
                          reads=[("f32p", 0)], writes=[("f32p", 0)])
                    S.add("vector", lambda e, eq=eq, hh=hh, pqs=pqs, texp=texp, b1=b1, bcol=bcol:
                          e.scalar_tensor_tensor(eq[:, hh*1024:(hh+1)*1024], pqs,
                                                 b1[:, bcol:bcol+1], texp, ALU.add, ALU.max),
                          reads=[pkey, ("f32p", 0)], writes=[ekey])
                if kind == "k":
                    S.add("vector", lambda e: e.reduce_sum(nrm[:, nt:nt+1], eq, AX.X),
                          reads=[ekey], writes=[("nrm", nt)])
            dst = {"q": qpt_h, "k": kpt_h, "v": vt_h}[kind]
            dkey = {"q": "qpt_h", "k": "kpt_h", "v": "vt_h"}[kind]
            S.add("sync", lambda e: e.dma_start(
                out=dst[nt*128:(nt+1)*128, :], in_=eq),
                reads=[ekey], writes=[(dkey, nt)])

        j = 0
        for g in range(G):
            proj_tile(j, "k", g); j += 1
        for g in range(G):
            proj_tile(j, "v", g); j += 1
        for nt in range(H):
            proj_tile(j, "q", nt); j += 1

        def rn_fn(e):
            e.tensor_scalar(rna[:], nrm[:], 1e-10, None, ALU.add)
            e.drain()
            return e.reciprocal(rnb[:], rna[:])
        S.add("vector", rn_fn, reads=[("nrm", g) for g in range(G)],
              writes=[("rnb",)])

        for i in range(NT):
            S.add("sync", lambda e, i=i: e.dma_start_transpose(
                big[:, i, 0:1024], kpt_h[:, i*128:(i+1)*128]),
                reads=[("kpt_h", g) for g in range(G)], writes=[("big", i)])
            S.add("sync", lambda e, i=i: e.dma_start_transpose(
                big[:, i, 1024:2048], vt_h[:, i*128:(i+1)*128]),
                reads=[("vt_h", g) for g in range(G)], writes=[("big", i)])

        for g in range(G):
            def kv_mm(e, g=g):
                for i in range(NT):
                    ii = e.matmul(pA[:, g*128:(g+1)*128],
                                  big[:, i, g*128:(g+1)*128],
                                  big[:, i, 1024+g*128:1024+(g+1)*128],
                                  start=(i == 0), stop=(i == NT - 1))
                return ii
            S.add("tensor", kv_mm, reads=[("big", i) for i in range(NT)],
                  writes=[("psum", 0)])
        for g in range(G):
            S.add("scalar", lambda e, g=g: e.activation(
                kvt[:, g*128:(g+1)*128], pA[:, g*128:(g+1)*128],
                AF.Identity, bias=0.0, scale=1.0),
                reads=[("psum", 0)], writes=[("kvt", g)])

        for h in range(H):
            g = h // 4
            qb = h % 2
            qbuf = b4[0][:, qb*2048:(qb+1)*2048]
            pa = psum[h % 2]
            S.add("sync", lambda e, h=h, qbuf=qbuf: e.dma_start(
                out=qbuf, in_=qpt_h[h*128:(h+1)*128, :]),
                reads=[("qpt_h", h)], writes=[("b4", 0, qb)])
            def amm(e, pa=pa, qbuf=qbuf, g=g):
                for tb in range(4):
                    ii = e.matmul(pa[:, tb*512:(tb+1)*512],
                                  kvt[:, g*128:(g+1)*128],
                                  qbuf[:, tb*512:(tb+1)*512],
                                  start=True, stop=True)
                return ii
            S.add("tensor", amm, reads=[("kvt", g), ("b4", 0, qb)],
                  writes=[("psum", h % 2)])
            S.add("scalar", lambda e, h=h, pa=pa, g=g: e.activation(
                big[:, h, :], pa[:], AF.Identity, bias=0.0, scale=rnb[:, g:g+1]),
                reads=[("psum", h % 2), ("rnb",)], writes=[("big", h)])

        for nt in range(FC):
            wb = nt % 2
            po = psum[nt % 2]
            S.add("gpsimd", lambda e, nt=nt, wb=wb: e.dma_start(
                out=b4[wb][:], in_=wot_h[nt, :, :]),
                writes=[("b4", wb, 0), ("b4", wb, 1)])
            def omm(e, po=po, wb=wb):
                for f in range(FC):
                    for tb in range(4):
                        ii = e.matmul(po[:, tb*512:(tb+1)*512],
                                      b4[wb][:, f*128:(f+1)*128],
                                      big[:, f, tb*512:(tb+1)*512],
                                      start=(f == 0), stop=(f == FC - 1))
                return ii
            S.add("tensor", omm,
                  reads=[("b4", wb, 0), ("b4", wb, 1)] + [("big", f) for f in range(FC)],
                  writes=[("psum", nt % 2)])
            obh = nt % 2
            ob = b4[2][:, obh*2048:(obh+1)*2048]
            S.add("scalar", lambda e, nt=nt, po=po, ob=ob: e.activation(
                ob, po[:], AF.Identity, bias=bol[:, nt:nt+1], scale=1.0),
                reads=[("psum", nt % 2), ("bol",)], writes=[("b4", 2, obh)])
            S.add("sync", lambda e, nt=nt, ob=ob: e.dma_start(
                out=o_h[nt*128:(nt+1)*128, :], in_=ob),
                reads=[("b4", 2, obh)], writes=[("o_h", nt)])

        for i in range(NT):
            xi = i % 2
            S.add("sync", lambda e, i=i, xi=xi: e.dma_start_transpose(
                b4[xi][:], o_h[:, i*128:(i+1)*128]),
                reads=[("o_h", nt) for nt in range(FC)],
                writes=[("b4", xi, 0), ("b4", xi, 1)])
            S.add("sync", lambda e, i=i, xi=xi: e.dma_start(
                out=f32p[xi][:], in_=x_h[i*128:(i+1)*128, :]),
                writes=[("f32p", xi)])
            ln_stats(i, b4[xi][:], [("b4", xi, 0), ("b4", xi, 1)], i, i % 2)
            def x2fn(e, i=i, xi=xi):
                e.scalar_tensor_tensor(f32p[xi][:], b4[xi][:], srb[:, i:i+1],
                                       f32p[xi][:], ALU.mult, ALU.add)
                e.drain()
                return e.tensor_scalar(f32p[xi][:], f32p[xi][:],
                                       snm[:, i:i+1], None, ALU.add)
            S.add("vector", x2fn,
                  reads=[("b4", xi, 0), ("b4", xi, 1), ("srb", i), ("snm", i), ("f32p", xi)],
                  writes=[("f32p", xi)])
            S.add("sync", lambda e, i=i, xi=xi: e.dma_start(
                out=x2_h[i*128:(i+1)*128, :], in_=f32p[xi][:]),
                reads=[("f32p", xi)], writes=[("x2_h", i)])
            ln_stats(i, f32p[xi][:], [("f32p", xi)], 16 + i, i % 2)
            def app2(e, i=i, xi=xi):
                for jj in range(4):
                    e.activation(b4[2][:, jj*1024:(jj+1)*1024],
                                 f32p[xi][:, jj::4], AF.Identity,
                                 bias=snm[:, 16+i:17+i], scale=srb[:, 16+i:17+i])
                return e.drain()
            S.add("scalar", app2,
                  reads=[("f32p", xi), ("srb", 16 + i), ("snm", 16 + i)],
                  writes=[("b4", 2, 0), ("b4", 2, 1)])
            S.add("sync", lambda e, i=i: e.dma_start(
                out=xn2_h[i*128:(i+1)*128, :], in_=b4[2][:]),
                reads=[("b4", 2, 0), ("b4", 2, 1)], writes=[("xn2_h", i)])

        S.add("sync", lambda e: e.dma_start(out=f32p[0][0:1, :], in_=zpr_h[:, :]),
              writes=[("f32p", 0)])
        S.add("vector", lambda e: e.tensor_copy(b4[3][0:1, :], f32p[0][0:1, :]),
              reads=[("f32p", 0)], writes=[("b4", 3, 0), ("b4", 3, 1)])

        for hh in range(2):
            for f in range(FC):
                S.add("sync", lambda e, f=f, hh=hh: e.dma_start_transpose(
                    big[:, f, hh*1024:(hh+1)*1024],
                    xn2_h[hh*1024:(hh+1)*1024, f*128:(f+1)*128]),
                    reads=[("xn2_h", i) for i in range(hh*8, hh*8+8)],
                    writes=[("big", f)])

        def rs_mm(e):
            for c in range(FC):
                for tb in range(4):
                    ii = e.matmul(pA[0:1, tb*512:(tb+1)*512], sones[:],
                                  big[:, c, tb*512:(tb+1)*512],
                                  start=(c == 0), stop=(c == FC - 1))
            return ii
        S.add("tensor", rs_mm,
              reads=[("sones",)] + [("big", c) for c in range(FC)],
              writes=[("psum", 0)])
        S.add("scalar", lambda e: e.activation(
            nrs_s, pA[0:1, 0:2048], AF.Identity, bias=0.0, scale=-1.0),
            reads=[("psum", 0)], writes=[("b4", 2, 1)])

        def emit_pb_unpack(nt):
            wb = nt % 2
            for ch in range(2):
                S.add("gpsimd", lambda e, nt=nt, ch=ch: e.dma_start(
                    out=pb_s[:], in_=pt_h[nt, ch, :, :]),
                    writes=[("pb",)])
                def unpack(e, wb=wb, ch=ch):
                    specs = [(0, 63, None, ALU.bitwise_and, ALU.bypass),
                             (0, 6, 63, ALU.logical_shift_right, ALU.bitwise_and),
                             (512, 63, None, ALU.bitwise_and, ALU.bypass),
                             (1024, 63, None, ALU.bitwise_and, ALU.bypass)]
                    for jj, (src, s1, s2, o1, o2) in enumerate(specs):
                        if s2 is None:
                            e.tensor_scalar(ub32[:], pb_s[:, src:src+512], s1, None, o1)
                        else:
                            e.tensor_scalar(ub32[:], pb_s[:, src:src+512], s1, s2, o1, o2)
                        e.drain()
                        e.tensor_copy(b4[wb][:, jj*1024+ch*512:jj*1024+ch*512+512], ub32[:])
                        e.drain()
                    return e.drain()
                S.add("vector", unpack, reads=[("pb",)],
                      writes=[("b4", wb, 0), ("b4", wb, 1), ("ub32",)])

        emit_pb_unpack(0)
        for nt in range(FC):
            wb = nt % 2
            pe_ = psum[nt % 2]
            if nt + 1 < FC:
                emit_pb_unpack(nt + 1)
            def fmm(e, pe_=pe_, wb=wb, nt=nt):
                for c in range(FC):
                    for tb in range(4):
                        e.matmul(pe_[:, tb*512:(tb+1)*512],
                                 b4[wb][:, c*128:(c+1)*128],
                                 big[:, c, tb*512:(tb+1)*512],
                                 start=(c == 0), stop=False)
                for tb in range(4):
                    ii = e.matmul(pe_[:, tb*512:(tb+1)*512],
                                  b4[3][0:1, nt*128:(nt+1)*128],
                                  b4[2][0:1, 2048+tb*512:2048+(tb+1)*512],
                                  start=False, stop=True)
                return ii
            S.add("tensor", fmm,
                  reads=[("b4", wb, 0), ("b4", wb, 1), ("b4", 3, 0), ("b4", 3, 1), ("b4", 2, 1)]
                        + [("big", c) for c in range(FC)],
                  writes=[("psum", nt % 2)])
            fb = b4[2][:, 0:2048]
            S.add("scalar", lambda e, nt=nt, pe_=pe_, fb=fb: e.activation(
                fb, pe_[:], AF.Identity, bias=bfl[:, nt:nt+1], scale=scl[:, nt:nt+1]),
                reads=[("psum", nt % 2), ("scl",), ("bfl",)], writes=[("b4", 2, 0)])
            S.add("sync", lambda e, nt=nt, fb=fb: e.dma_start(
                out=ffn_h[nt*128:(nt+1)*128, :], in_=fb),
                reads=[("b4", 2, 0)], writes=[("ffn_h", nt)])

        for i in range(NT):
            xi = i % 2
            S.add("sync", lambda e, i=i, xi=xi: e.dma_start_transpose(
                b4[xi][:], ffn_h[:, i*128:(i+1)*128]),
                reads=[("ffn_h", nt) for nt in range(FC)],
                writes=[("b4", xi, 0), ("b4", xi, 1)])
            S.add("sync", lambda e, i=i, xi=xi: e.dma_start(
                out=f32p[xi][:], in_=x2_h[i*128:(i+1)*128, :]),
                reads=[("x2_h", i)], writes=[("f32p", xi)])
            ln_stats(i, b4[xi][:], [("b4", xi, 0), ("b4", xi, 1)], i, i % 2)
            def outfn(e, i=i, xi=xi):
                e.scalar_tensor_tensor(f32p[xi][:], b4[xi][:], srb[:, i:i+1],
                                       f32p[xi][:], ALU.mult, ALU.add)
                e.drain()
                return e.tensor_scalar(f32p[xi][:], f32p[xi][:],
                                       snm[:, i:i+1], None, ALU.add)
            S.add("vector", outfn,
                  reads=[("b4", xi, 0), ("b4", xi, 1), ("srb", i), ("snm", i), ("f32p", xi)],
                  writes=[("f32p", xi)])
            S.add("sync", lambda e, i=i, xi=xi: e.dma_start(
                out=out_h[i*128:(i+1)*128, :], in_=f32p[xi][:]),
                reads=[("f32p", xi)], writes=[("out_h", i)])

        S.emit(nc, block, sems)
    return nc


def kernel(**inputs):
    x = np.asarray(inputs["L0_attn_hidden_states"])
    Wq = np.asarray(inputs["L0_attn_Wq"]); bq = np.asarray(inputs["L0_attn_bq"])
    Wk = np.asarray(inputs["L0_attn_Wk"]); bk = np.asarray(inputs["L0_attn_bk"])
    Wv = np.asarray(inputs["L0_attn_Wv"]); bv = np.asarray(inputs["L0_attn_bv"])
    Wo = np.asarray(inputs["L0_attn_Wo"]); bo = np.asarray(inputs["L0_attn_bo"])
    qw = np.asarray(inputs["L0_ffn_qweight"])
    wsc = np.asarray(inputs["L0_ffn_weight_scale"])
    wzp = np.asarray(inputs["L0_ffn_weight_zero_point"])
    bf_ = np.asarray(inputs["L0_ffn_bias"])

    C = np.ascontiguousarray

    def wtile(W, n_tiles):
        return C(W.reshape(n_tiles, 128, 32, 128).transpose(0, 3, 2, 1)
                  .reshape(n_tiles, 128, 4096))

    wqt = wtile(Wq, 32)
    wkt = wtile(Wk, 8)
    wvt = wtile(Wv, 8)
    wot = wtile(Wo, 32)
    PT_all = qw.reshape(4096, 1024, 3).transpose(2, 1, 0)
    pt = C(PT_all.reshape(3, 2, 4, 128, 32, 128).transpose(4, 1, 3, 0, 2, 5)
                 .reshape(32, 2, 128, 1536))

    percol = lambda v: C(v.reshape(-1, 128).T)

    shared = {
        "wqt": wqt, "wkt": wkt, "wvt": wvt, "wot": wot, "pt": pt,
        "bqv": percol(bq), "bkv": percol(bk), "bvv": percol(bv),
        "bov": percol(bo), "scv": percol(wsc.reshape(-1)),
        "bfv": percol(bf_), "zpr": C(wzp.reshape(1, 4096)),
    }
    nc = build_nc()
    in_maps = [dict(shared, x=C(x[b])) for b in range(8)]
    res = run_bass_kernel_spmd(nc, in_maps, core_ids=list(range(8)))
    out = np.stack([np.asarray(res.results[b]["out"]) for b in range(8)], axis=0)
    return out.astype(np.float32)


# version 09 (final)
# speedup vs baseline: 1.1403x; 1.1403x over previous
import numpy as np
from contextlib import ExitStack

import concourse.bass as bass
import concourse.mybir as mybir
from concourse.bass_utils import run_bass_kernel_spmd

F32 = mybir.dt.float32
BF16 = mybir.dt.bfloat16
I32 = mybir.dt.int32
AF = mybir.ActivationFunctionType
ALU = mybir.AluOpType
AX = mybir.AxisListType

T = 2048
D = 4096
NT = T // 128
FC = D // 128
G = 8
H = 32
DKV = 1024
EPS = 1e-5

COMPUTE = ("tensor", "vector", "scalar")
QUEUES = ("tensor", "vector", "scalar", "gpsimd", "sync", "sdma")
PHYS = {"tensor": "tensor", "vector": "vector", "scalar": "scalar",
        "gpsimd": "gpsimd", "sync": "sync", "sdma": "scalar"}


class Sched:

    def __init__(self):
        self.tasks = []
        self.last_writer = {}
        self.readers = {}

    def add(self, engine, emit, reads=(), writes=(), inc=None):
        tid = len(self.tasks)
        if inc is None:
            inc = 16 if engine in ("sync", "gpsimd", "sdma") else 1
        deps = set()
        for k in reads:
            w = self.last_writer.get(k)
            if w is not None:
                deps.add(w)
        for k in writes:
            w = self.last_writer.get(k)
            if w is not None:
                deps.add(w)
            for rr in self.readers.get(k, ()):
                deps.add(rr)
        deps.discard(tid)
        for k in reads:
            self.readers.setdefault(k, set()).add(tid)
        for k in writes:
            self.last_writer[k] = tid
            self.readers[k] = set()
        self.tasks.append((engine, inc, emit, deps))
        return tid

    def emit(self, nc, block, sems):
        cum = {q: 0 for q in QUEUES}
        info = []
        for (queue, inc, emit, deps) in self.tasks:
            cum[queue] += inc
            info.append((queue, cum[queue]))

        totals = dict(cum)

        def run_engine(phys):
            myqueues = [q for q in QUEUES if PHYS[q] == phys]
            def f(e):
                waited = {q: 0 for q in QUEUES}
                for tid, (queue, inc, emit, deps) in enumerate(self.tasks):
                    if queue not in myqueues:
                        continue
                    need_drain = False
                    for d in sorted(deps):
                        dq, dcum = info[d]
                        if dq in COMPUTE and PHYS[dq] == phys and queue in COMPUTE:
                            need_drain = True
                        else:
                            if dcum > waited[dq]:
                                waited[dq] = dcum
                                e.wait_ge(sems[dq], dcum)
                    if need_drain:
                        e.drain()
                    last = emit(e)
                    last.then_inc(sems[queue], inc)
                if phys == "sync":
                    for q in QUEUES:
                        e.wait_ge(sems[q], totals[q])
            return f

        block.sync(run_engine("sync"))
        block.vector(run_engine("vector"))
        block.scalar(run_engine("scalar"))
        block.tensor(run_engine("tensor"))
        block.gpsimd(run_engine("gpsimd"))


def build_nc():
    nc = bass.Bass("TRN2", target_bir_lowering=False)

    x_h = nc.dram_tensor("x", [T, D], F32, kind="ExternalInput").ap()
    wqt_h = nc.dram_tensor("wqt", [32, 128, 4096], F32, kind="ExternalInput").ap()
    wkt_h = nc.dram_tensor("wkt", [8, 128, 4096], F32, kind="ExternalInput").ap()
    wvt_h = nc.dram_tensor("wvt", [8, 128, 4096], F32, kind="ExternalInput").ap()
    wot_h = nc.dram_tensor("wot", [32, 128, 4096], F32, kind="ExternalInput").ap()
    pt_h = nc.dram_tensor("pt", [32, 2, 128, 1536], I32, kind="ExternalInput").ap()
    bqv_h = nc.dram_tensor("bqv", [128, 32], F32, kind="ExternalInput").ap()
    bkv_h = nc.dram_tensor("bkv", [128, 8], F32, kind="ExternalInput").ap()
    bvv_h = nc.dram_tensor("bvv", [128, 8], F32, kind="ExternalInput").ap()
    bov_h = nc.dram_tensor("bov", [128, 32], F32, kind="ExternalInput").ap()
    scv_h = nc.dram_tensor("scv", [128, 32], F32, kind="ExternalInput").ap()
    bfv_h = nc.dram_tensor("bfv", [128, 32], F32, kind="ExternalInput").ap()
    zpr_h = nc.dram_tensor("zpr", [1, 4096], F32, kind="ExternalInput").ap()

    out_h = nc.dram_tensor("out", [T, D], F32, kind="ExternalOutput").ap()
    xn1_h = nc.dram_tensor("xn1_h", [T, D], BF16, kind="Internal").ap()
    qpt_h = nc.dram_tensor("qpt_h", [D, T], BF16, kind="Internal").ap()
    kpt_h = nc.dram_tensor("kpt_h", [DKV, T], BF16, kind="Internal").ap()
    vt_h = nc.dram_tensor("vt_h", [DKV, T], BF16, kind="Internal").ap()
    o_h = nc.dram_tensor("o_h", [D, T], BF16, kind="Internal").ap()
    x2_h = nc.dram_tensor("x2_h", [T, D], F32, kind="Internal").ap()
    xn2_h = nc.dram_tensor("xn2_h", [T, D], BF16, kind="Internal").ap()
    ffn_h = nc.dram_tensor("ffn_h", [D, T], BF16, kind="Internal").ap()

    with ExitStack() as stack:
        sb = lambda *a: stack.enter_context(nc.sbuf_tensor(*a))
        ps = lambda *a: stack.enter_context(nc.psum_tensor(*a))

        big = sb("big", [128, FC, T], BF16)
        b4 = [sb(f"b4_{u}", [128, 4096], BF16) for u in range(4)]
        f32p = [sb(f"f32p_{u}", [128, 4096], F32) for u in range(2)]
        pb_s = sb("pb", [128, 1536], I32)
        ub32 = sb("ub32", [128, 512], I32)
        nrs_s = None

        bn6 = [sb(f"bn6_{u}", [128, 48], F32) for u in range(2)]
        mvb = sb("mvb", [128, 64], F32)
        vpe = sb("vpe", [128, 32], F32)
        sdv = sb("sdv", [128, 32], F32)
        srb = sb("srb", [128, 32], F32)
        snm = sb("snm", [128, 32], F32)
        sones = sb("sones", [128, 1], BF16)
        bql = sb("bql", [128, 32], F32)
        bq1l = sb("bq1l", [128, 32], F32)
        bkl = sb("bkl", [128, 8], F32)
        bk1l = sb("bk1l", [128, 8], F32)
        bvl = sb("bvl", [128, 8], F32)
        bol = sb("bol", [128, 32], F32)
        scl = sb("scl", [128, 32], F32)
        bfl = sb("bfl", [128, 32], F32)
        nrm = sb("nrm", [128, 8], F32)
        rna = sb("rna", [128, 8], F32)
        rnb = sb("rnb", [128, 8], F32)
        kvt = sb("kvt", [128, 1024], BF16)

        nrs_s = b4[2][0:1, 2048:4096]

        pA = ps("pA", [128, 2048], F32)
        pB = ps("pB", [128, 2048], F32)
        psum = [pA, pB]

        sems = {q: stack.enter_context(nc.semaphore(f"sem_{q}")) for q in QUEUES}
        block = stack.enter_context(nc.Block())

        S = Sched()

        S.add("sync", lambda e: e.dma_start(out=bql[:], in_=bqv_h[:, :]), writes=[("bql",)])
        S.add("sync", lambda e: e.dma_start(out=bkl[:], in_=bkv_h[:, :]), writes=[("bkl",)])
        S.add("sync", lambda e: e.dma_start(out=bvl[:], in_=bvv_h[:, :]), writes=[("bvl",)])
        S.add("sync", lambda e: e.dma_start(out=bol[:], in_=bov_h[:, :]), writes=[("bol",)])
        S.add("sync", lambda e: e.dma_start(out=scl[:], in_=scv_h[:, :]), writes=[("scl",)])
        S.add("sync", lambda e: e.dma_start(out=bfl[:], in_=bfv_h[:, :]), writes=[("bfl",)])
        S.add("vector", lambda e: e.tensor_scalar(bq1l[:], bql[:], 1.0, None, ALU.add),
              reads=[("bql",)], writes=[("bq1l",)])
        S.add("vector", lambda e: e.tensor_scalar(bk1l[:], bkl[:], 1.0, None, ALU.add),
              reads=[("bkl",)], writes=[("bk1l",)])
        S.add("vector", lambda e: e.memset(sones[:], 1.0), writes=[("sones",)])

        def ln_stats(i, src_ap, src_keys, col, bnbuf):
            c = col
            def bn_fn(e):
                for g in range(8):
                    e.bn_stats(bn6[bnbuf][:, g*6:(g+1)*6], src_ap[:, g*512:(g+1)*512])
                e.drain()
                e.bn_aggr(mvb[:, 2*c:2*c+2], bn6[bnbuf][:])
                e.drain()
                return e.tensor_scalar(vpe[:, c:c+1], mvb[:, 2*c+1:2*c+2], EPS, None, ALU.add)
            S.add("vector", bn_fn, reads=src_keys,
                  writes=[("bn6", bnbuf), ("mvb", c), ("vpe", c)])
            S.add("scalar", lambda e: e.activation(sdv[:, c:c+1], vpe[:, c:c+1], AF.Sqrt),
                  reads=[("vpe", c)], writes=[("sdv", c)])
            def fin(e):
                e.reciprocal(srb[:, c:c+1], sdv[:, c:c+1])
                e.drain()
                return e.tensor_scalar(snm[:, c:c+1], mvb[:, 2*c:2*c+1],
                                       srb[:, c:c+1], -1.0, ALU.mult, ALU.mult)
            S.add("vector", fin, reads=[("sdv", c), ("mvb", c)],
                  writes=[("srb", c), ("snm", c)])

        pending = []
        for i in range(NT):
            xi = i % 2
            ui = 2 + (i % 2)
            S.add("sync", lambda e, i=i, xi=xi: e.dma_start(
                out=f32p[xi][:], in_=x_h[i*128:(i+1)*128, :]),
                reads=[], writes=[("f32p", xi)])
            for fn in pending:
                fn()
            pending = []
            ln_stats(i, f32p[xi][:], [("f32p", xi)], i, i % 2)
            S.add("scalar", lambda e, i=i, xi=xi, ui=ui: e.activation(
                b4[ui][:], f32p[xi][:], AF.Identity,
                bias=snm[:, i:i+1], scale=srb[:, i:i+1]),
                reads=[("f32p", xi), ("srb", i), ("snm", i)],
                writes=[("b4", ui, 0), ("b4", ui, 1)])
            pending.append(lambda i=i, ui=ui: S.add(
                "sync", lambda e, i=i, ui=ui: e.dma_start(
                    out=xn1_h[i*128:(i+1)*128, :], in_=b4[ui][:]),
                reads=[("b4", ui, 0), ("b4", ui, 1)], writes=[("xn1_h", i)]))
        for fn in pending:
            fn()

        for hh in range(2):
            for f in range(FC):
                S.add("sync", lambda e, f=f, hh=hh: e.dma_start_transpose(
                    big[:, f, hh*1024:(hh+1)*1024],
                    xn1_h[hh*1024:(hh+1)*1024, f*128:(f+1)*128]),
                    reads=[("xn1_h", i) for i in range(hh*8, hh*8+8)],
                    writes=[("big", f)])

        def proj_tile(j, kind, nt):
            wsrc = {"q": wqt_h, "k": wkt_h, "v": wvt_h}[kind]
            wb = j % 2
            pq = psum[j % 2]
            pkey = ("psum", j % 2)
            S.add("gpsimd", lambda e: e.dma_start(out=b4[wb][:], in_=wsrc[nt, :, :]),
                  writes=[("b4", wb, 0), ("b4", wb, 1)])
            def mm(e):
                for k in range(FC):
                    for tb in range(4):
                        ii = e.matmul(pq[:, tb*512:(tb+1)*512],
                                      b4[wb][:, k*128:(k+1)*128],
                                      big[:, k, tb*512:(tb+1)*512],
                                      start=(k == 0), stop=(k == FC - 1))
                return ii
            S.add("tensor", mm,
                  reads=[("b4", wb, 0), ("b4", wb, 1)] + [("big", k) for k in range(FC)],
                  writes=[pkey])
            eqh = j % 2
            eq = b4[2][:, eqh*2048:(eqh+1)*2048]
            ekey = ("b4", 2, eqh)
            if kind == "v":
                S.add("scalar", lambda e: e.activation(
                    eq, pq[:], AF.Identity, bias=bvl[:, nt:nt+1], scale=1.0),
                    reads=[pkey, ("bvl",)], writes=[ekey])
            else:
                bl = bql if kind == "q" else bkl
                b1 = bq1l if kind == "q" else bk1l
                bcol = nt
                for hh in range(2):
                    tmin = f32p[0][:, hh*2048:hh*2048+1024]
                    texp = f32p[0][:, hh*2048+1024:hh*2048+2048]
                    pqs = pq[:, hh*1024:(hh+1)*1024]
                    S.add("vector", lambda e, tmin=tmin, pqs=pqs, bl=bl, bcol=bcol:
                          e.tensor_scalar(tmin, pqs, bl[:, bcol:bcol+1], 0.0, ALU.add, ALU.min),
                          reads=[pkey], writes=[("f32p", 0)])
                    S.add("scalar", lambda e, texp=texp, tmin=tmin:
                          e.activation(texp, tmin, AF.Exp),
                          reads=[("f32p", 0)], writes=[("f32p", 0)])
                    S.add("vector", lambda e, eq=eq, hh=hh, pqs=pqs, texp=texp, b1=b1, bcol=bcol:
                          e.scalar_tensor_tensor(eq[:, hh*1024:(hh+1)*1024], pqs,
                                                 b1[:, bcol:bcol+1], texp, ALU.add, ALU.max),
                          reads=[pkey, ("f32p", 0)], writes=[ekey])
                if kind == "k":
                    S.add("vector", lambda e: e.reduce_sum(nrm[:, nt:nt+1], eq, AX.X),
                          reads=[ekey], writes=[("nrm", nt)])
            dst = {"q": qpt_h, "k": kpt_h, "v": vt_h}[kind]
            dkey = {"q": "qpt_h", "k": "kpt_h", "v": "vt_h"}[kind]
            S.add("sync", lambda e: e.dma_start(
                out=dst[nt*128:(nt+1)*128, :], in_=eq),
                reads=[ekey], writes=[(dkey, nt)])

        j = 0
        for g in range(G):
            proj_tile(j, "k", g); j += 1
        for g in range(G):
            proj_tile(j, "v", g); j += 1
        for nt in range(H):
            proj_tile(j, "q", nt); j += 1

        def rn_fn(e):
            e.tensor_scalar(rna[:], nrm[:], 1e-10, None, ALU.add)
            e.drain()
            return e.reciprocal(rnb[:], rna[:])
        S.add("vector", rn_fn, reads=[("nrm", g) for g in range(G)],
              writes=[("rnb",)])

        for i in range(NT):
            S.add("sync", lambda e, i=i: e.dma_start_transpose(
                big[:, i, 0:1024], kpt_h[:, i*128:(i+1)*128]),
                reads=[("kpt_h", g) for g in range(G)], writes=[("big", i)])
            S.add("sync", lambda e, i=i: e.dma_start_transpose(
                big[:, i, 1024:2048], vt_h[:, i*128:(i+1)*128]),
                reads=[("vt_h", g) for g in range(G)], writes=[("big", i)])

        for g in range(G):
            def kv_mm(e, g=g):
                for i in range(NT):
                    ii = e.matmul(pA[:, g*128:(g+1)*128],
                                  big[:, i, g*128:(g+1)*128],
                                  big[:, i, 1024+g*128:1024+(g+1)*128],
                                  start=(i == 0), stop=(i == NT - 1))
                return ii
            S.add("tensor", kv_mm, reads=[("big", i) for i in range(NT)],
                  writes=[("psum", 0)])
        for g in range(G):
            S.add("scalar", lambda e, g=g: e.activation(
                kvt[:, g*128:(g+1)*128], pA[:, g*128:(g+1)*128],
                AF.Identity, bias=0.0, scale=1.0),
                reads=[("psum", 0)], writes=[("kvt", g)])

        for h in range(H):
            g = h // 4
            qb = h % 2
            qbuf = b4[0][:, qb*2048:(qb+1)*2048]
            pa = psum[h % 2]
            S.add("sync", lambda e, h=h, qbuf=qbuf: e.dma_start(
                out=qbuf, in_=qpt_h[h*128:(h+1)*128, :]),
                reads=[("qpt_h", h)], writes=[("b4", 0, qb)])
            def amm(e, pa=pa, qbuf=qbuf, g=g):
                for tb in range(4):
                    ii = e.matmul(pa[:, tb*512:(tb+1)*512],
                                  kvt[:, g*128:(g+1)*128],
                                  qbuf[:, tb*512:(tb+1)*512],
                                  start=True, stop=True)
                return ii
            S.add("tensor", amm, reads=[("kvt", g), ("b4", 0, qb)],
                  writes=[("psum", h % 2)])
            S.add("scalar", lambda e, h=h, pa=pa, g=g: e.activation(
                big[:, h, :], pa[:], AF.Identity, bias=0.0, scale=rnb[:, g:g+1]),
                reads=[("psum", h % 2), ("rnb",)], writes=[("big", h)])

        for nt in range(FC):
            wb = nt % 2
            po = psum[nt % 2]
            S.add("gpsimd", lambda e, nt=nt, wb=wb: e.dma_start(
                out=b4[wb][:], in_=wot_h[nt, :, :]),
                writes=[("b4", wb, 0), ("b4", wb, 1)])
            def omm(e, po=po, wb=wb):
                for f in range(FC):
                    for tb in range(4):
                        ii = e.matmul(po[:, tb*512:(tb+1)*512],
                                      b4[wb][:, f*128:(f+1)*128],
                                      big[:, f, tb*512:(tb+1)*512],
                                      start=(f == 0), stop=(f == FC - 1))
                return ii
            S.add("tensor", omm,
                  reads=[("b4", wb, 0), ("b4", wb, 1)] + [("big", f) for f in range(FC)],
                  writes=[("psum", nt % 2)])
            obh = nt % 2
            ob = b4[2][:, obh*2048:(obh+1)*2048]
            S.add("scalar", lambda e, nt=nt, po=po, ob=ob: e.activation(
                ob, po[:], AF.Identity, bias=bol[:, nt:nt+1], scale=1.0),
                reads=[("psum", nt % 2), ("bol",)], writes=[("b4", 2, obh)])
            S.add("sync", lambda e, nt=nt, ob=ob: e.dma_start(
                out=o_h[nt*128:(nt+1)*128, :], in_=ob),
                reads=[("b4", 2, obh)], writes=[("o_h", nt)])

        pending = []
        for i in range(NT):
            xi = i % 2
            S.add("sync", lambda e, i=i, xi=xi: e.dma_start_transpose(
                b4[xi][:], o_h[:, i*128:(i+1)*128]),
                reads=[("o_h", nt) for nt in range(FC)],
                writes=[("b4", xi, 0), ("b4", xi, 1)])
            S.add("sync", lambda e, i=i, xi=xi: e.dma_start(
                out=f32p[xi][:], in_=x_h[i*128:(i+1)*128, :]),
                writes=[("f32p", xi)])
            for fn in pending:
                fn()
            pending = []
            ln_stats(i, b4[xi][:], [("b4", xi, 0), ("b4", xi, 1)], i, i % 2)
            def x2fn(e, i=i, xi=xi):
                e.scalar_tensor_tensor(f32p[xi][:], b4[xi][:], srb[:, i:i+1],
                                       f32p[xi][:], ALU.mult, ALU.add)
                e.drain()
                return e.tensor_scalar(f32p[xi][:], f32p[xi][:],
                                       snm[:, i:i+1], None, ALU.add)
            S.add("vector", x2fn,
                  reads=[("b4", xi, 0), ("b4", xi, 1), ("srb", i), ("snm", i), ("f32p", xi)],
                  writes=[("f32p", xi)])
            pending.append(lambda i=i, xi=xi: S.add(
                "sync", lambda e, i=i, xi=xi: e.dma_start(
                    out=x2_h[i*128:(i+1)*128, :], in_=f32p[xi][:]),
                reads=[("f32p", xi)], writes=[("x2_h", i)]))
            ln_stats(i, f32p[xi][:], [("f32p", xi)], 16 + i, i % 2)
            def app2(e, i=i, xi=xi):
                for jj in range(4):
                    ii = e.activation(b4[2][:, jj*1024:(jj+1)*1024],
                                      f32p[xi][:, jj::4], AF.Identity,
                                      bias=snm[:, 16+i:17+i], scale=srb[:, 16+i:17+i])
                return ii
            S.add("scalar", app2,
                  reads=[("f32p", xi), ("srb", 16 + i), ("snm", 16 + i)],
                  writes=[("b4", 2, 0), ("b4", 2, 1)])
            pending.append(lambda i=i: S.add(
                "sync", lambda e, i=i: e.dma_start(
                    out=xn2_h[i*128:(i+1)*128, :], in_=b4[2][:]),
                reads=[("b4", 2, 0), ("b4", 2, 1)], writes=[("xn2_h", i)]))
        for fn in pending:
            fn()

        S.add("sync", lambda e: e.dma_start(out=f32p[0][0:1, :], in_=zpr_h[:, :]),
              writes=[("f32p", 0)])
        S.add("vector", lambda e: e.tensor_copy(b4[3][0:1, :], f32p[0][0:1, :]),
              reads=[("f32p", 0)], writes=[("b4", 3, 0), ("b4", 3, 1)])

        for hh in range(2):
            for f in range(FC):
                S.add("sync", lambda e, f=f, hh=hh: e.dma_start_transpose(
                    big[:, f, hh*1024:(hh+1)*1024],
                    xn2_h[hh*1024:(hh+1)*1024, f*128:(f+1)*128]),
                    reads=[("xn2_h", i) for i in range(hh*8, hh*8+8)],
                    writes=[("big", f)])

        def rs_mm(e):
            for c in range(FC):
                for tb in range(4):
                    ii = e.matmul(pA[0:1, tb*512:(tb+1)*512], sones[:],
                                  big[:, c, tb*512:(tb+1)*512],
                                  start=(c == 0), stop=(c == FC - 1))
            return ii
        S.add("tensor", rs_mm,
              reads=[("sones",)] + [("big", c) for c in range(FC)],
              writes=[("psum", 0)])
        S.add("scalar", lambda e: e.activation(
            nrs_s, pA[0:1, 0:2048], AF.Identity, bias=0.0, scale=-1.0),
            reads=[("psum", 0)], writes=[("b4", 2, 1)])

        def emit_pb_unpack(nt):
            wb = nt % 2
            for ch in range(2):
                S.add("gpsimd", lambda e, nt=nt, ch=ch: e.dma_start(
                    out=pb_s[:], in_=pt_h[nt, ch, :, :]),
                    writes=[("pb",)])
                def unpack(e, wb=wb, ch=ch):
                    specs = [(0, 63, None, ALU.bitwise_and, ALU.bypass),
                             (0, 6, 63, ALU.logical_shift_right, ALU.bitwise_and),
                             (512, 63, None, ALU.bitwise_and, ALU.bypass),
                             (1024, 63, None, ALU.bitwise_and, ALU.bypass)]
                    for jj, (src, s1, s2, o1, o2) in enumerate(specs):
                        if s2 is None:
                            e.tensor_scalar(ub32[:], pb_s[:, src:src+512], s1, None, o1)
                        else:
                            e.tensor_scalar(ub32[:], pb_s[:, src:src+512], s1, s2, o1, o2)
                        e.drain()
                        ii = e.tensor_copy(b4[wb][:, jj*1024+ch*512:jj*1024+ch*512+512], ub32[:])
                        e.drain()
                    return ii
                S.add("vector", unpack, reads=[("pb",)],
                      writes=[("b4", wb, 0), ("b4", wb, 1), ("ub32",)])

        emit_pb_unpack(0)
        for nt in range(FC):
            wb = nt % 2
            pe_ = psum[nt % 2]
            if nt + 1 < FC:
                emit_pb_unpack(nt + 1)
            def fmm(e, pe_=pe_, wb=wb, nt=nt):
                for c in range(FC):
                    for tb in range(4):
                        e.matmul(pe_[:, tb*512:(tb+1)*512],
                                 b4[wb][:, c*128:(c+1)*128],
                                 big[:, c, tb*512:(tb+1)*512],
                                 start=(c == 0), stop=False)
                for tb in range(4):
                    ii = e.matmul(pe_[:, tb*512:(tb+1)*512],
                                  b4[3][0:1, nt*128:(nt+1)*128],
                                  b4[2][0:1, 2048+tb*512:2048+(tb+1)*512],
                                  start=False, stop=True)
                return ii
            S.add("tensor", fmm,
                  reads=[("b4", wb, 0), ("b4", wb, 1), ("b4", 3, 0), ("b4", 3, 1), ("b4", 2, 1)]
                        + [("big", c) for c in range(FC)],
                  writes=[("psum", nt % 2)])
            fb = b4[2][:, 0:2048]
            S.add("scalar", lambda e, nt=nt, pe_=pe_, fb=fb: e.activation(
                fb, pe_[:], AF.Identity, bias=bfl[:, nt:nt+1], scale=scl[:, nt:nt+1]),
                reads=[("psum", nt % 2), ("scl",), ("bfl",)], writes=[("b4", 2, 0)])
            S.add("sync", lambda e, nt=nt, fb=fb: e.dma_start(
                out=ffn_h[nt*128:(nt+1)*128, :], in_=fb),
                reads=[("b4", 2, 0)], writes=[("ffn_h", nt)])

        pending = []
        for i in range(NT):
            xi = i % 2
            S.add("sync", lambda e, i=i, xi=xi: e.dma_start_transpose(
                b4[xi][:], ffn_h[:, i*128:(i+1)*128]),
                reads=[("ffn_h", nt) for nt in range(FC)],
                writes=[("b4", xi, 0), ("b4", xi, 1)])
            S.add("sync", lambda e, i=i, xi=xi: e.dma_start(
                out=f32p[xi][:], in_=x2_h[i*128:(i+1)*128, :]),
                reads=[("x2_h", i)], writes=[("f32p", xi)])
            for fn in pending:
                fn()
            pending = []
            ln_stats(i, b4[xi][:], [("b4", xi, 0), ("b4", xi, 1)], i, i % 2)
            def outfn(e, i=i, xi=xi):
                e.scalar_tensor_tensor(f32p[xi][:], b4[xi][:], srb[:, i:i+1],
                                       f32p[xi][:], ALU.mult, ALU.add)
                e.drain()
                return e.tensor_scalar(f32p[xi][:], f32p[xi][:],
                                       snm[:, i:i+1], None, ALU.add)
            S.add("vector", outfn,
                  reads=[("b4", xi, 0), ("b4", xi, 1), ("srb", i), ("snm", i), ("f32p", xi)],
                  writes=[("f32p", xi)])
            pending.append(lambda i=i, xi=xi: S.add(
                "sync", lambda e, i=i, xi=xi: e.dma_start(
                    out=out_h[i*128:(i+1)*128, :], in_=f32p[xi][:]),
                reads=[("f32p", xi)], writes=[("out_h", i)]))
        for fn in pending:
            fn()

        S.emit(nc, block, sems)
    return nc


def kernel(**inputs):
    x = np.asarray(inputs["L0_attn_hidden_states"])
    Wq = np.asarray(inputs["L0_attn_Wq"]); bq = np.asarray(inputs["L0_attn_bq"])
    Wk = np.asarray(inputs["L0_attn_Wk"]); bk = np.asarray(inputs["L0_attn_bk"])
    Wv = np.asarray(inputs["L0_attn_Wv"]); bv = np.asarray(inputs["L0_attn_bv"])
    Wo = np.asarray(inputs["L0_attn_Wo"]); bo = np.asarray(inputs["L0_attn_bo"])
    qw = np.asarray(inputs["L0_ffn_qweight"])
    wsc = np.asarray(inputs["L0_ffn_weight_scale"])
    wzp = np.asarray(inputs["L0_ffn_weight_zero_point"])
    bf_ = np.asarray(inputs["L0_ffn_bias"])

    C = np.ascontiguousarray

    def wtile(W, n_tiles):
        return C(W.reshape(n_tiles, 128, 32, 128).transpose(0, 3, 2, 1)
                  .reshape(n_tiles, 128, 4096))

    wqt = wtile(Wq, 32)
    wkt = wtile(Wk, 8)
    wvt = wtile(Wv, 8)
    wot = wtile(Wo, 32)
    PT_all = qw.reshape(4096, 1024, 3).transpose(2, 1, 0)
    pt = C(PT_all.reshape(3, 2, 4, 128, 32, 128).transpose(4, 1, 3, 0, 2, 5)
                 .reshape(32, 2, 128, 1536))

    percol = lambda v: C(v.reshape(-1, 128).T)

    shared = {
        "wqt": wqt, "wkt": wkt, "wvt": wvt, "wot": wot, "pt": pt,
        "bqv": percol(bq), "bkv": percol(bk), "bvv": percol(bv),
        "bov": percol(bo), "scv": percol(wsc.reshape(-1)),
        "bfv": percol(bf_), "zpr": C(wzp.reshape(1, 4096)),
    }
    nc = build_nc()
    in_maps = [dict(shared, x=C(x[b])) for b in range(8)]
    res = run_bass_kernel_spmd(nc, in_maps, core_ids=list(range(8)))
    out = np.stack([np.asarray(res.results[b]["out"]) for b in range(8)], axis=0)
    return out.astype(np.float32)
